# version 02
# baseline (speedup 1.0000x reference)
import numpy as np
import ml_dtypes
from contextlib import ExitStack
import concourse.bass as bass
import concourse.mybir as mybir
from concourse.bass_utils import run_bass_kernel_spmd

F32 = mybir.dt.float32
BF16 = mybir.dt.bfloat16
AF = mybir.ActivationFunctionType
ALU = mybir.AluOpType

NCORE = 8
DM = 1024
DI = 2048
NST = 16
DTR = 64
FF = 4096
TS = 512
WIN = 4096
HW = 2048
DSL = DI // NCORE
EPS = 1e-5

ENGINES = ("sync", "scalar", "vector", "tensor", "gpsimd")
_DRAM_REG = {}
NJ_LIM = NJ = 8
P1_STEPS = {"x", "ln1", "mm1", "mm2", "stash", "ln2", "out"}
P2_STEPS = {"ipj", "conv", "xp", "dt", "wt", "scan", "gate"}
NJ2_LIM = 8


class Prog:
    def __init__(self, nc, es):
        self.nc = nc
        self.streams = {e: [] for e in ENGINES}
        self.semh = {}
        self.count = {}
        for s in ("dma", "pe", "act", "dve", "gp"):
            self.semh[s] = es.enter_context(nc.semaphore("sem_" + s))
            self.count[s] = 0

    def op(self, eng, fn, waits=(), inc=None, amt=1):
        flat = []
        for w in waits:
            if w is None:
                continue
            if isinstance(w, list):
                flat.extend(x for x in w if x is not None)
            else:
                flat.append(w)
        waits = tuple(flat)
        tok = None
        if inc is not None:
            self.count[inc] += amt
            tok = (inc, self.count[inc])
        semh = self.semh

        def run(e, _fn=fn, _waits=waits, _inc=inc, _amt=amt):
            for s, v in _waits:
                e.wait_ge(semh[s], v)
            inst = _fn(e)
            if _inc is not None and inst is not None:
                inst.then_inc(semh[_inc], _amt)
        self.streams[eng].append(run)
        return tok

    def tok(self, name):
        return (name, self.count[name])


def build_program(phases=("p1", "ag1", "p2", "p3"), debug_outs=()):
    nc = bass.Bass("TRN2", target_bir_lowering=False)

    def din(name, shape, dt=F32):
        return nc.dram_tensor(name, list(shape), dt, kind="ExternalInput").ap()

    def dout(name, shape, dt=F32):
        return nc.dram_tensor(name, list(shape), dt, kind="ExternalOutput").ap()

    def dint(name, shape, dt=F32, shared=False):
        kw = dict(addr_space="Shared") if shared else {}
        ap = nc.dram_tensor(name, list(shape), dt, kind="Internal", **kw).ap()
        _DRAM_REG[name] = ap
        return ap

    xs = din("xs", [NJ, TS, DM])
    w1 = din("w1", [DM, FF])
    w2 = din("w2", [FF, DM])
    winT = din("winT", [DM, 2 * DSL])
    wxpT = din("wxpT", [DSL, 96])
    wdtT = din("wdtT", [DTR, DSL])
    woutT = din("woutT", [DI, DM])
    b1r = din("b1r", [128, 32])
    b2r = din("b2r", [128, 8])
    convw_r = din("convw_r", [128, 8])
    alog_r = din("alog_r", [128, 32])
    cols_r = din("cols_r", [128, 8])
    ident_in = din("ident", [128, 128])
    ys = dout("ys", [NJ, TS, DM])

    w1b = dint("w1b", [DM, FF], BF16)
    w2b = dint("w2b", [FF, DM], BF16)
    winTb = dint("winTb", [DM, 2 * DSL], BF16)
    wxpTb = dint("wxpTb", [DSL, 96], BF16)
    wdtTb = dint("wdtTb", [DTR, DSL], BF16)
    woutTb = dint("woutTb", [DI, DM], BF16)
    x1_scr = dint("x1_scr", [NJ, 128, FF])
    ln2_in = [dint(f"ln2_in{j}", [DM, TS], BF16) for j in range(NJ)]
    ln2_out = [dint(f"ln2_out{j}", [NCORE * DM, TS], BF16, shared=True) for j in range(NJ)]
    xp_in = [dint(f"xp_in{j}", [96, WIN]) for j in range(NJ)]
    xp_out = [dint(f"xp_out{j}", [96, WIN], shared=True) for j in range(NJ)]
    y_in = [dint(f"y_in{j}", [2 * 128, WIN], BF16) for j in range(NJ)]
    y_out = [dint(f"y_out{j}", [DI, WIN], BF16, shared=True) for j in range(NJ)]

    es = ExitStack()
    def sb(name, shape, dt=F32):
        t = es.enter_context(nc.sbuf_tensor(name, list(shape), dt))
        _DRAM_REG["sb_" + name] = t[:]
        return t
    pst = lambda name, shape: es.enter_context(nc.psum_tensor(name, list(shape), F32))

    big0 = sb("big0", [128, 16400])
    big1 = sb("big1", [128, 16384])
    med0 = sb("med0", [128, 8192])
    med1 = sb("med1", [128, 4096])
    sm0 = sb("sm0", [128, 2048])
    sm1 = sb("sm1", [128, 2048])
    sm2 = sb("sm2", [128, 1024])
    sm3 = sb("sm3", [128, 512])
    sm4 = sb("sm4", [128, 512])
    ident = sb("identt", [128, 128])
    onescol = sb("onescol", [128, 1])
    onesrow = sb("onesrow", [1, 128])
    carry = sb("carry", [128, 2 * NST])
    convtail = sb("convtail", [128, 8])
    smallp = sb("smallp", [128, 64])
    apar = sb("apar", [128, 64])

    wxp_sb = sb("wxp_sb", [128, 2 * 96], BF16)
    wdt_sb = sb("wdt_sb", [DTR, DSL], BF16)


    ps_mm = pst("ps_mm", [128, 2048])
    ps_rep = pst("ps_rep", [128, 1024])
    ps_tr = pst("ps_tr", [128, 1024])

    w1b_t = big0[:, 0:16384].bitcast(BF16).rearrange("p (ck f) -> p ck f", ck=8)
    w2b_t = big1[:, 0:16384].bitcast(BF16).rearrange("p (oc f) -> p oc f", oc=32)
    hb_t = med0[:, 0:8192].bitcast(BF16).rearrange("p (oc f) -> p oc f", oc=32)
    lnb_t = sm0[:, 0:2048].bitcast(BF16).rearrange("p (ck f) -> p ck f", ck=8)
    xfm = med1
    xsq = med0[:, 0:4096]
    rows = med0[0:1, 4096:6144]
    dtin_f = med1[0:64, 0:2048]
    dtin_b = med1[0:64, 2048:3072].bitcast(BF16)
    yb_stage = med0[:, 4096:8192].bitcast(BF16)
    wstream = sm1[:, 0:2048].bitcast(BF16)

    xin3 = big0[:, 0:8200].rearrange("p (g t) -> p g t", g=2)
    res3 = big0[:, 8200:16392].rearrange("p (g t) -> p g t", g=2)
    u3 = big1[:, 0:8192].rearrange("p (g t) -> p g t", g=2)
    wt3 = big1[:, 8192:16384].rearrange("p (g t) -> p g t", g=2)
    sc = [med0[:, k * 2048:(k + 1) * 2048] for k in range(4)]
    crep_v = sm0[:, 0:2048]
    ya = [med1[:, 0:2048], med1[:, 2048:4096]]
    winTb_t = sm1[:, 0:2048].bitcast(BF16).rearrange("p (ck er) -> p ck er", ck=8)
    stage_t = big1[:, 0:2048].bitcast(BF16).rearrange("p (ck f) -> p ck f", ck=8)

    prog = Prog(nc, es)
    P = prog.op
    T = prog.tok

    def DMA(fn, waits=()):
        return P("sync", fn, waits, inc="dma", amt=16)

    def PE(fn, waits=()):
        return P("tensor", fn, waits, inc="pe")

    def ACT(fn, waits=()):
        return P("scalar", fn, waits, inc="act")

    def DVE(fn, waits=()):
        return P("vector", fn, waits, inc="dve")

    def COLL(fn, waits=()):
        return P("gpsimd", fn, waits, inc="gp", amt=1)

    def WAITALL(eng):
        waits = [T(s) for s in ("dma", "pe", "act", "dve", "gp")]
        P(eng, lambda e: None, waits)

    def BARRIER():
        for e in ENGINES:
            WAITALL(e)

    DMA(lambda e: e.dma_start(out=ident[:], in_=ident_in))
    DVE(lambda e: e.memset(onescol[:], 1.0))
    DVE(lambda e: e.memset(onesrow[:], 1.0))
    DMA(lambda e: e.dma_start(out=apar[:, 0:32], in_=alog_r))
    DMA(lambda e: e.dma_start(out=apar[:, 32:40], in_=convw_r))
    DMA(lambda e: e.dma_start(out=smallp[:, 0:32], in_=b1r))
    DMA(lambda e: e.dma_start(out=sm3[:, 0:8], in_=b2r))
    DMA(lambda e: e.dma_start(out=smallp[:, 40:48], in_=cols_r))
    t0 = T("dma")
    ACT(lambda e: e.activation(out=apar[:, 0:32], in_=apar[:, 0:32], func=AF.Exp), waits=[t0])
    DVE(lambda e: e.tensor_scalar_mul(out=apar[:, 0:32], in0=apar[:, 0:32], scalar1=-1.0), waits=[T("act")])
    DVE(lambda e: e.tensor_scalar_mul(out=smallp[:, 32:40], in0=sm3[:, 0:8], scalar1=0.5), waits=[t0])
    DVE(lambda e: e.memset(smallp[:, 48:49], EPS))
    DVE(lambda e: e.memset(smallp[:, 49:50], 1.0))
    nc.const_aps.register(F32, EPS, smallp[:, 48:49])
    nc.const_aps.register(F32, 1.0, smallp[:, 49:50])
    t_small = T("dve")

    A_col = lambda dg, n: apar[:, dg * 16 + n:dg * 16 + n + 1]
    cw_col = lambda dg, k: apar[:, 32 + dg * 4 + k:33 + dg * 4 + k]
    b1_col = lambda oc: smallp[:, oc:oc + 1]
    b2h_col = lambda mc: smallp[:, 32 + mc:33 + mc]
    cb_col = lambda dg: smallp[:, 40 + dg:41 + dg]
    dtb_col = lambda dg: smallp[:, 42 + dg:43 + dg]
    dp_col = lambda dg: smallp[:, 44 + dg:45 + dg]

    cbf = med0[:, 0:4096]
    cbb = med1[:, 0:2048].bitcast(BF16)
    last_cast_read = [None]

    def cast_weight(src, dst, nrows, ncols, scale=None):
        for rc in range(-(-nrows // 128)):
            r0 = rc * 128
            pr = min(128, nrows - r0)
            for cc in range(-(-ncols // 4096)):
                c0 = cc * 4096
                fc = min(4096, ncols - c0)
                tl = DMA(lambda e, r0=r0, pr=pr, c0=c0, fc=fc: e.dma_start(
                    out=cbf[0:pr, 0:fc], in_=src[r0:r0 + pr, c0:c0 + fc]),
                    waits=[last_cast_read[0]])
                if scale is None:
                    tc_ = DVE(lambda e, pr=pr, fc=fc: e.tensor_copy(
                        out=cbb[0:pr, 0:fc], in_=cbf[0:pr, 0:fc]), waits=[tl])
                else:
                    tc_ = DVE(lambda e, pr=pr, fc=fc, s=scale: e.tensor_scalar_mul(
                        out=cbb[0:pr, 0:fc], in0=cbf[0:pr, 0:fc], scalar1=s), waits=[tl])
                DMA(lambda e, r0=r0, pr=pr, c0=c0, fc=fc: e.dma_start(
                    out=dst[r0:r0 + pr, c0:c0 + fc], in_=cbb[0:pr, 0:fc]), waits=[tc_])
                last_cast_read[0] = tc_
        return T("dma")

    cast_weight(w1, w1b, DM, FF)
    cast_weight(w2, w2b, FF, DM, scale=0.5)
    cast_weight(winT, winTb, DM, 2 * DSL)
    cast_weight(wxpT, wxpTb, DSL, 96)
    cast_weight(wdtT, wdtTb, DTR, DSL)
    t_wcast = cast_weight(woutT, woutTb, DI, DM)

    st = {"stats_read": None, "rep_read": None, "tr_evac": [None] * 4,
          "mm_evac": [None] * 4, "sq_read": None}

    def layernorm(dep_toks, dest_lnb):
        deps = list(dep_toks)
        for ck in range(8):
            PE(lambda e, ck=ck: e.matmul(ps_rep[0:1, 0:512], onescol[:], xfm[:, ck * 512:(ck + 1) * 512],
                                         start=(ck == 0), stop=(ck == 7)),
               waits=deps + [st["stats_read"]])
        for ck in range(8):
            tq = ACT(lambda e, ck=ck: e.activation(out=xsq[:, ck * 512:(ck + 1) * 512],
                                                   in_=xfm[:, ck * 512:(ck + 1) * 512], func=AF.Square),
                     waits=deps + [st["sq_read"]])
            PE(lambda e, ck=ck: e.matmul(ps_rep[0:1, 512:1024], onescol[:], xsq[:, ck * 512:(ck + 1) * 512],
                                         start=(ck == 0), stop=(ck == 7)), waits=[tq])
        t_stats = T("pe")
        rm, rv, rr, rmr = rows[0:1, 0:512], rows[0:1, 512:1024], rows[0:1, 1024:1536], rows[0:1, 1536:2048]
        DVE(lambda e: e.tensor_scalar_mul(out=rm, in0=ps_rep[0:1, 0:512], scalar1=1.0 / DM), waits=[t_stats])
        DVE(lambda e: e.tensor_scalar_mul(out=rv, in0=ps_rep[0:1, 512:1024], scalar1=1.0 / DM), waits=[t_stats])
        st["stats_read"] = T("dve")
        DVE(lambda e: e.tensor_mul(out=rr, in0=rm, in1=rm))
        DVE(lambda e: e.tensor_sub(out=rv, in0=rv, in1=rr), waits=[T("dve")])
        t4 = T("dve")
        t5 = ACT(lambda e: e.activation(out=rr, in_=rv, func=AF.Ln, bias=EPS), waits=[t4])
        t6 = ACT(lambda e: e.activation(out=rr, in_=rr, func=AF.Exp, scale=-0.5), waits=[t5])
        t7 = DVE(lambda e: e.tensor_mul(out=rmr, in0=rm, in1=rr), waits=[t6])
        t8 = PE(lambda e: e.matmul(ps_rep[:, 0:512], onesrow[:], rr, start=True, stop=True),
                waits=[t6, st["rep_read"]])
        t9 = PE(lambda e: e.matmul(ps_rep[:, 512:1024], onesrow[:], rmr, start=True, stop=True),
                waits=[t7, st["rep_read"]])
        tn = None
        for ck in range(8):
            ta = DVE(lambda e, ck=ck: e.tensor_mul(out=xsq[:, ck * 512:(ck + 1) * 512],
                                                   in0=xfm[:, ck * 512:(ck + 1) * 512],
                                                   in1=ps_rep[:, 0:512]), waits=[t8, t9])
            tn = DVE(lambda e, ck=ck: e.tensor_sub(out=dest_lnb[:, ck, :],
                                                   in0=xsq[:, ck * 512:(ck + 1) * 512],
                                                   in1=ps_rep[:, 512:1024]), waits=[ta])
        st["rep_read"] = tn
        st["sq_read"] = tn
        return tn

    BARRIER()
    if "p1" not in phases:
        _finish(nc, prog, es, dram_dbg(nc, prog, DMA, debug_outs), [])
        return nc
    tlw1 = DMA(lambda e: e.dma_start(out=w1b_t[:, :, :],
                                     in_=w1b.rearrange("(ck p) f -> p ck f", p=128)), waits=[t_wcast])
    tlw2 = DMA(lambda e: e.dma_start(out=w2b_t[:, :, :],
                                     in_=w2b.rearrange("(oc p) f -> p oc f", p=128)), waits=[t_wcast])

    t_prev_ln2 = None
    t_prev_stash = None
    t_p1_lnout = []
    for j in range(NJ_LIM):
        t_tr_last = None
        for tt in range(4):
            tl = DMA(lambda e, j=j, tt=tt: e.dma_start(out=sm2[:, 0:1024],
                                                       in_=xs[j, tt * 128:(tt + 1) * 128, :]),
                     waits=[t_tr_last, t_prev_ln2])
            for ck in range(8):
                slot = ck % 2
                tp = PE(lambda e, ck=ck, slot=slot: e.transpose(
                    ps_tr[:, slot * 512:slot * 512 + 128], sm2[:, ck * 128:(ck + 1) * 128], ident[:]),
                    waits=[tl, t0, st["tr_evac"][slot]])
                st["tr_evac"][slot] = ACT(lambda e, ck=ck, tt=tt, slot=slot: e.copy(
                    out=xfm[:, ck * 512 + tt * 128: ck * 512 + (tt + 1) * 128],
                    in_=ps_tr[:, slot * 512:slot * 512 + 128]), waits=[tp])
            t_tr_last = T("pe")
        t_x = T("act")
        if "ln1" not in P1_STEPS:
            continue
        t_ln1 = layernorm([t_x], lnb_t)
        if "mm1" not in P1_STEPS:
            continue
        for oc in range(32):
            bank = oc % 4
            for ck in range(8):
                PE(lambda e, oc=oc, ck=ck, bank=bank: e.matmul(
                    ps_mm[:, bank * 512:(bank + 1) * 512],
                    w1b_t[:, ck, oc * 128:(oc + 1) * 128], lnb_t[:, ck, :],
                    start=(ck == 0), stop=(ck == 7)),
                   waits=[t_ln1, tlw1, st["mm_evac"][bank]] if ck == 0 else [])
            tmm = T("pe")
            st["mm_evac"][bank] = ACT(lambda e, oc=oc, bank=bank: e.activation(
                out=hb_t[:, oc, :], in_=ps_mm[:, bank * 512:(bank + 1) * 512],
                func=AF.Relu, bias=b1_col(oc)), waits=[tmm, t_small])
        t_h = T("act")
        if "mm2" not in P1_STEPS:
            continue
        for mc in range(8):
            bank = mc % 4
            for oc in range(32):
                PE(lambda e, mc=mc, oc=oc, bank=bank: e.matmul(
                    ps_mm[:, bank * 512:(bank + 1) * 512],
                    w2b_t[:, oc, mc * 128:(mc + 1) * 128], hb_t[:, oc, :],
                    start=(oc == 0), stop=(oc == 31)),
                   waits=[t_h, tlw2, st["mm_evac"][bank]] if oc == 0 else [])
            tmm = T("pe")
            st["mm_evac"][bank] = DVE(lambda e, mc=mc, bank=bank: e.scalar_tensor_tensor(
                out=xfm[:, mc * 512:(mc + 1) * 512],
                in0=ps_mm[:, bank * 512:(bank + 1) * 512], scalar=b2h_col(mc),
                in1=xfm[:, mc * 512:(mc + 1) * 512], op0=ALU.add, op1=ALU.add),
                waits=[tmm, t_small])
        t_x1 = T("dve")
        if "stash" not in P1_STEPS:
            continue
        t_prev_stash = DMA(lambda e, j=j: e.dma_start(out=x1_scr[j], in_=xfm[:, 0:4096]), waits=[t_x1])
        if "ln2" not in P1_STEPS:
            continue
        t_ln2 = layernorm([t_x1], lnb_t)
        tout = DMA(lambda e, j=j: e.dma_start(
            out=ln2_in[j].rearrange("(ck p) t -> p ck t", p=128),
            in_=lnb_t[:, :, :]), waits=[t_ln2])
        t_p1_lnout.append(tout)
        t_prev_ln2 = t_ln2

    if "ag1" not in phases:
        _finish(nc, prog, es, dram_dbg(nc, prog, DMA, debug_outs), [])
        return nc
    t_ag = []
    for j in range(NJ):
        t_ag.append(COLL(lambda e, j=j: e.collective_compute(
            "AllGather", ALU.bypass, replica_groups=[list(range(NCORE))],
            ins=[ln2_in[j].opt()], outs=[ln2_out[j].opt()]), waits=[t_p1_lnout[j], t_prev_stash]))

    if "p2" not in phases:
        _finish(nc, prog, es, dram_dbg(nc, prog, DMA, debug_outs), [])
        return nc
    BARRIER()
    tlwin = DMA(lambda e: e.dma_start(out=winTb_t[:, :, :],
                                      in_=winTb.rearrange("(ck p) er -> p ck er", p=128)), waits=[t_wcast])
    tlwxp = DMA(lambda e: e.dma_start(out=wxp_sb.rearrange("p (g r) -> p g r", g=2),
                                      in_=wxpTb.rearrange("(g p) r -> p g r", p=128)),
                waits=[t_wcast])
    tlwdt = DMA(lambda e: e.dma_start(out=wdt_sb[:], in_=wdtTb), waits=[t_wcast])

    t_yag = []
    _p2full = {"ipj", "conv", "xp", "dt", "wt", "scan", "gate"} <= P2_STEPS
    prev = {"xin_read": None, "res_read": None, "u_read": None, "wt_read": None,
            "stage_read": None, "scan": [None] * 6, "dtin_read": None, "ya_read": [None, None],
            "xp_evac": None, "ygb_read": None, "dtadd": None}
    for j in range(min(NJ, NJ2_LIM)):
        batch_start = (j % 4 == 0)
        if batch_start:
            t_tail = DVE(lambda e: e.memset(xin3[:, :, 0:3], 0.0), waits=[prev["xin_read"]])
            t_cm = DVE(lambda e: e.memset(carry[:], 0.0))
        else:
            t_tail = DVE(lambda e: e.tensor_copy(out=xin3[:, :, 0:3],
                                                 in_=convtail.rearrange("p (g k) -> p g k", g=2)[:, :, 0:3]),
                         waits=[prev["xin_read"]])
            t_cm = None
        if "ipj" not in P2_STEPS:
            continue
        for sct in range(8):
            tsg = DMA(lambda e, j=j, sct=sct: e.dma_start(
                out=stage_t[:, :, :],
                in_=ln2_out[j][sct * DM:(sct + 1) * DM, :].rearrange("(ck p) t -> p ck t", p=128)),
                waits=[t_ag[j], tlwin, prev["stage_read"], prev["u_read"], prev["ygb_read"]])
            for er in range(4):
                bank = er % 4
                for ck in range(8):
                    PE(lambda e, er=er, ck=ck, bank=bank: e.matmul(
                        ps_mm[:, bank * 512:(bank + 1) * 512],
                        winTb_t[:, ck, er * 128:(er + 1) * 128], stage_t[:, ck, :],
                        start=(ck == 0), stop=(ck == 7)),
                       waits=[tsg, st["mm_evac"][bank]] if ck == 0 else [])
                tmm = T("pe")
                if er < 2:
                    st["mm_evac"][bank] = ACT(lambda e, er=er, sct=sct, bank=bank: e.copy(
                        out=xin3[:, er, 3 + sct * 512: 3 + (sct + 1) * 512],
                        in_=ps_mm[:, bank * 512:(bank + 1) * 512]), waits=[tmm, t_tail])
                else:
                    st["mm_evac"][bank] = ACT(lambda e, er=er, sct=sct, bank=bank: e.copy(
                        out=res3[:, er - 2, sct * 512:(sct + 1) * 512],
                        in_=ps_mm[:, bank * 512:(bank + 1) * 512]), waits=[tmm, prev["res_read"]])
            prev["stage_read"] = T("pe")
        t_ip = T("act")
        t_ct = DVE(lambda e: e.tensor_copy(out=convtail.rearrange("p (g k) -> p g k", g=2)[:, :, 0:3],
                                           in_=xin3[:, :, 4096:4099]),
                   waits=[t_ip])
        if "conv" not in P2_STEPS:
            continue
        t_u = []
        for dg in range(2):
            cacc = med0[:, dg * 4096:(dg + 1) * 4096]
            t1 = DVE(lambda e, dg=dg, cacc=cacc: e.tensor_scalar(
                out=cacc, in0=xin3[:, dg, 0:4096], scalar1=cw_col(dg, 0), scalar2=cb_col(dg),
                op0=ALU.mult, op1=ALU.add), waits=[t_ip, t_small, prev["scan"][0], prev["scan"][1],
                                                   prev["scan"][2], prev["scan"][3]])
            for k in range(1, 4):
                t1 = DVE(lambda e, dg=dg, k=k, cacc=cacc: e.scalar_tensor_tensor(
                    out=cacc, in0=xin3[:, dg, k:k + 4096], scalar=cw_col(dg, k),
                    in1=cacc, op0=ALU.mult, op1=ALU.add), waits=[t1])
            t_u.append(ACT(lambda e, dg=dg, cacc=cacc: e.activation(
                out=u3[:, dg, :], in_=cacc, func=AF.Silu), waits=[t1, prev["u_read"]]))
        if "xp" not in P2_STEPS:
            continue
        ub_t = sm2[:, 0:1024].bitcast(BF16)
        t_xp_last = None
        for sct in range(8):
            tc1 = DVE(lambda e, sct=sct: e.tensor_copy(out=ub_t[:, 0:512], in_=u3[:, 0, sct * 512:(sct + 1) * 512]),
                      waits=[t_u[0], prev["xp_evac"], prev["ygb_read"]])
            tc2 = DVE(lambda e, sct=sct: e.tensor_copy(out=ub_t[:, 512:1024], in_=u3[:, 1, sct * 512:(sct + 1) * 512]),
                      waits=[t_u[1]])
            for dg in range(2):
                PE(lambda e, dg=dg: e.matmul(ps_tr[0:96, 0:512], wxp_sb[:, dg * 96:(dg + 1) * 96],
                                             ub_t[:, dg * 512:(dg + 1) * 512],
                                             start=(dg == 0), stop=(dg == 1)),
                   waits=[tc1, tc2, tlwxp, prev["xp_evac"]] if dg == 0 else [])
            tmm = T("pe")
            prev["xp_evac"] = ACT(lambda e: e.copy(out=sm4[0:96, :], in_=ps_tr[0:96, 0:512]), waits=[tmm])
            t_xp_last = DMA(lambda e, j=j, sct=sct: e.dma_start(
                out=xp_in[j][:, sct * 512:(sct + 1) * 512], in_=sm4[0:96, :]), waits=[prev["xp_evac"]])
        t_ar = COLL(lambda e, j=j: e.collective_compute(
            "AllReduce", ALU.add, replica_groups=[list(range(NCORE))],
            ins=[xp_in[j].opt()], outs=[xp_out[j].opt()]), waits=[t_xp_last])
        if "dt" not in P2_STEPS:
            continue
        t_ygb_dma = None
        for h in range(2):
            toff = h * HW
            tdl = DMA(lambda e, j=j, toff=toff: e.dma_start(
                out=dtin_f[:, :], in_=xp_out[j][0:DTR, toff:toff + HW]),
                waits=[t_ar, prev["dtin_read"], prev["ya_read"][0], prev["ya_read"][1]])
            tdc = DVE(lambda e: e.tensor_copy(out=dtin_b[:, 0:HW], in_=dtin_f[:, :]), waits=[tdl])
            prev["dtin_read"] = tdc
            for dg in range(2):
                for tc in range(4):
                    bank = tc % 4
                    tm = PE(lambda e, dg=dg, tc=tc, bank=bank: e.matmul(
                        ps_mm[:, bank * 512:(bank + 1) * 512],
                        wdt_sb[:, dg * 128:(dg + 1) * 128], dtin_b[0:DTR, tc * 512:(tc + 1) * 512],
                        start=True, stop=True), waits=[tdc, tlwdt, st["mm_evac"][bank]])
                    txb = DVE(lambda e, bank=bank, dg=dg: e.tensor_scalar_add(
                        out=sm3[:], in0=ps_mm[:, bank * 512:(bank + 1) * 512],
                        scalar1=dtb_col(dg)), waits=[tm, prev.get("dtadd"), st.get("sp_m")])
                    st["mm_evac"][bank] = txb
                    tmx = DVE(lambda e: e.tensor_scalar_max(out=sm2[:, 512:1024], in0=sm3[:], scalar1=0.0),
                              waits=[txb, st.get("sp_l")])
                    st["sp_m"] = tmx
                    ta1 = ACT(lambda e: e.activation(out=sm4[:], in_=sm3[:], func=AF.Abs),
                              waits=[txb, prev.get("dtadd")])
                    ta2 = ACT(lambda e: e.activation(out=sm2[:, 0:512], in_=sm4[:], func=AF.Exp, scale=-1.0),
                              waits=[ta1, prev["ygb_read"]])
                    ta3 = ACT(lambda e: e.activation(out=sm4[:], in_=sm2[:, 0:512], func=AF.Ln, bias=1.0),
                              waits=[ta2])
                    st["sp_l"] = ta3
                    prev["dtadd"] = DVE(lambda e, dg=dg, tc=tc, toff=toff: e.tensor_add(
                        out=xin3[:, dg, toff + tc * 512: toff + (tc + 1) * 512],
                        in0=sm2[:, 512:1024], in1=sm4[:]), waits=[ta3, tmx])
            t_dt = T("dve")
            if "wt" not in P2_STEPS:
                continue
            for dg in range(2):
                DVE(lambda e, dg=dg, toff=toff: e.tensor_mul(
                    out=wt3[:, dg, toff:toff + HW], in0=xin3[:, dg, toff:toff + HW],
                    in1=u3[:, dg, toff:toff + HW]), waits=[t_dt, prev["wt_read"]])
            t_wt = T("dve")
            if "scan" not in P2_STEPS:
                continue
            for n in range(NST):
                tb = DMA(lambda e, j=j, n=n, toff=toff: e.dma_start(
                    out=sc[3], in_=xp_out[j][64 + n:65 + n, toff:toff + HW].partition_broadcast(128)),
                    waits=[t_ar, prev["scan"][3]])
                tcp = DMA(lambda e, j=j, n=n, toff=toff: e.dma_start(
                    out=crep_v, in_=xp_out[j][80 + n:81 + n, toff:toff + HW].partition_broadcast(128)),
                    waits=[t_ar, prev["scan"][4]])
                for dg in range(2):
                    tda = ACT(lambda e, dg=dg, n=n, toff=toff: e.activation(
                        out=sc[0], in_=xin3[:, dg, toff:toff + HW], func=AF.Exp, scale=A_col(dg, n)),
                        waits=[t_dt, prev["scan"][0], t_small])
                    tbb = DVE(lambda e, dg=dg, toff=toff: e.tensor_mul(
                        out=sc[1], in0=wt3[:, dg, toff:toff + HW], in1=sc[3]),
                        waits=[t_wt, tb, prev["scan"][1]])
                    tts = DVE(lambda e, dg=dg, n=n: e.tensor_tensor_scan(
                        out=sc[2], data0=sc[0], data1=sc[1],
                        initial=carry[:, dg * NST + n:dg * NST + n + 1],
                        op0=ALU.mult, op1=ALU.add), waits=[tda, tbb, prev["scan"][2], t_cm])
                    tcs = DVE(lambda e, dg=dg, n=n: e.tensor_copy(
                        out=carry[:, dg * NST + n:dg * NST + n + 1], in_=sc[2][:, HW - 1:HW]), waits=[tts])
                    tgg = DVE(lambda e: e.tensor_mul(out=sc[0], in0=sc[2], in1=crep_v),
                              waits=[tts, tcp])
                    prev["scan"][0] = tgg
                    prev["scan"][1] = tts
                    prev["scan"][2] = tgg
                    prev["scan"][3] = tbb
                    prev["scan"][4] = tgg
                    if n == 0:
                        tya = DVE(lambda e, dg=dg: e.tensor_copy(out=ya[dg], in_=sc[0]),
                                  waits=[tgg, prev["ya_read"][dg]])
                    else:
                        tya = DVE(lambda e, dg=dg: e.tensor_add(out=ya[dg], in0=ya[dg], in1=sc[0]),
                                  waits=[tgg])
                    prev["scan"][0] = tya
                    prev["scan"][5] = tya
            if "gate" not in P2_STEPS:
                continue
            for dg in range(2):
                tsi = ACT(lambda e, dg=dg, toff=toff: e.activation(
                    out=sc[1], in_=res3[:, dg, toff:toff + HW], func=AF.Silu),
                    waits=[prev["scan"][1], prev["scan"][5]])
                tg1 = DVE(lambda e, dg=dg, toff=toff: e.scalar_tensor_tensor(
                    out=sc[0], in0=u3[:, dg, toff:toff + HW], scalar=dp_col(dg),
                    in1=ya[dg], op0=ALU.mult, op1=ALU.add), waits=[prev["scan"][5], prev["scan"][0]])
                tg2 = DVE(lambda e: e.tensor_mul(out=sm2[:, 0:1024].bitcast(BF16)[:, 0:2048],
                                                 in0=sc[0], in1=sc[1]), waits=[tg1, tsi, prev["ygb_read"]])
                prev["ya_read"][dg] = tg1
                prev["scan"][0] = tg2
                prev["scan"][1] = tg2
                t_ygb_dma = DMA(lambda e, j=j, dg=dg, toff=toff: e.dma_start(
                    out=y_in[j][dg * 128:(dg + 1) * 128, toff:toff + HW],
                    in_=sm2[:, 0:1024].bitcast(BF16)[:, 0:2048]), waits=[tg2])
                prev["ygb_read"] = t_ygb_dma
            prev["u_read"] = tg1
            prev["res_read"] = tsi
            prev["wt_read"] = T("dve")
            prev["xin_read"] = T("act")
        if _p2full:
            t_yag.append(COLL(lambda e, j=j: e.collective_compute(
                "AllGather", ALU.bypass, replica_groups=[list(range(NCORE))],
                ins=[y_in[j].opt()], outs=[y_out[j].opt()]), waits=[t_ygb_dma]))

    if "p3" not in phases:
        _finish(nc, prog, es, dram_dbg(nc, prog, DMA, debug_outs), [])
        return nc
    BARRIER()
    tlw1b = DMA(lambda e: e.dma_start(out=w1b_t[:, :, :],
                                      in_=w1b.rearrange("(ck p) f -> p ck f", p=128)))
    tlw2b = DMA(lambda e: e.dma_start(out=w2b_t[:, :, :],
                                      in_=w2b.rearrange("(oc p) f -> p oc f", p=128)))
    yb_t = yb_stage.rearrange("p (ec t) -> p ec t", ec=16)
    wout_t = wstream.rearrange("p (s ec f) -> p s ec f", s=2, ec=16)
    t_prev_out = None
    for j in range(NJ):
        tyb = DMA(lambda e, j=j: e.dma_start(
            out=yb_t[:, :, :],
            in_=y_out[j].rearrange("(ec p) t -> p ec t", p=128)[:, :, bass.ts(nc.partition_id(), TS)]),
            waits=[t_yag[j], t_prev_out])
        tx1 = DMA(lambda e, j=j: e.dma_start(out=xfm[:, 0:4096], in_=x1_scr[j]),
                  waits=[t_prev_out])
        for mc in range(8):
            bank = mc % 4
            slot = mc % 2
            tws = DMA(lambda e, mc=mc, slot=slot: e.dma_start(
                out=wout_t[:, slot, :, :],
                in_=woutTb.rearrange("(ec p) o -> p ec o", p=128)[:, :, mc * 128:(mc + 1) * 128]),
                waits=[st.get("wout_read_%d" % slot)])
            for ec in range(16):
                PE(lambda e, mc=mc, ec=ec, bank=bank, slot=slot: e.matmul(
                    ps_mm[:, bank * 512:(bank + 1) * 512],
                    wout_t[:, slot, ec, :], yb_t[:, ec, :],
                    start=(ec == 0), stop=(ec == 15)),
                   waits=[tws, tyb, st["mm_evac"][bank]] if ec == 0 else [])
            tmm = T("pe")
            st["wout_read_%d" % slot] = tmm
            st["mm_evac"][bank] = DVE(lambda e, mc=mc, bank=bank: e.tensor_add(
                out=xfm[:, mc * 512:(mc + 1) * 512],
                in0=ps_mm[:, bank * 512:(bank + 1) * 512],
                in1=xfm[:, mc * 512:(mc + 1) * 512]), waits=[tmm, tx1])
        t_x2 = T("dve")
        t_ln3 = layernorm([t_x2], lnb_t)
        for oc in range(32):
            bank = oc % 4
            for ck in range(8):
                PE(lambda e, oc=oc, ck=ck, bank=bank: e.matmul(
                    ps_mm[:, bank * 512:(bank + 1) * 512],
                    w1b_t[:, ck, oc * 128:(oc + 1) * 128], lnb_t[:, ck, :],
                    start=(ck == 0), stop=(ck == 7)),
                   waits=[t_ln3, tlw1b, st["mm_evac"][bank]] if ck == 0 else [])
            tmm = T("pe")
            st["mm_evac"][bank] = ACT(lambda e, oc=oc, bank=bank: e.activation(
                out=hb_t[:, oc, :], in_=ps_mm[:, bank * 512:(bank + 1) * 512],
                func=AF.Relu, bias=b1_col(oc)), waits=[tmm])
        t_h = T("act")
        for mc in range(8):
            bank = mc % 4
            for oc in range(32):
                PE(lambda e, mc=mc, oc=oc, bank=bank: e.matmul(
                    ps_mm[:, bank * 512:(bank + 1) * 512],
                    w2b_t[:, oc, mc * 128:(mc + 1) * 128], hb_t[:, oc, :],
                    start=(oc == 0), stop=(oc == 31)),
                   waits=[t_h, tlw2b, st["mm_evac"][bank]] if oc == 0 else [])
            tmm = T("pe")
            st["mm_evac"][bank] = DVE(lambda e, mc=mc, bank=bank: e.scalar_tensor_tensor(
                out=xfm[:, mc * 512:(mc + 1) * 512],
                in0=ps_mm[:, bank * 512:(bank + 1) * 512], scalar=b2h_col(mc),
                in1=xfm[:, mc * 512:(mc + 1) * 512], op0=ALU.add, op1=ALU.add), waits=[tmm])
        t_out = T("dve")
        for tt in range(4):
            tev = None
            for ck in range(8):
                slot = ck % 2
                tp = PE(lambda e, ck=ck, tt=tt, slot=slot: e.transpose(
                    ps_tr[:, slot * 512:slot * 512 + 128],
                    xfm[:, ck * 512 + tt * 128: ck * 512 + (tt + 1) * 128], ident[:]),
                    waits=[t_out, st["tr_evac"][slot], t_prev_out])
                st["tr_evac"][slot] = tev = ACT(lambda e, ck=ck, slot=slot: e.copy(
                    out=sm2[:, ck * 128:(ck + 1) * 128], in_=ps_tr[:, slot * 512:slot * 512 + 128]),
                    waits=[tp])
            t_prev_out = DMA(lambda e, j=j, tt=tt: e.dma_start(
                out=ys[j, tt * 128:(tt + 1) * 128, :], in_=sm2[:, 0:1024]), waits=[tev])
        prev["stage_read"] = None

    _finish(nc, prog, es, dram_dbg(nc, prog, DMA, debug_outs), [])
    return nc


def dram_dbg(nc, prog, DMA, debug_outs):
    return [("dbg_" + n, _DRAM_REG[n]) for n in debug_outs]


def _finish(nc, prog, es, dbg, _unused):
    def DMA(fn, waits=()):
        return prog.op("sync", fn, waits, inc="dma", amt=16)
    if dbg:
        for outname, ap in dbg:
            od = nc.dram_tensor(outname, list(ap.shape), ap.dtype, kind="ExternalOutput").ap()
            DMA(lambda e, od=od, ap=ap: e.dma_start(out=od, in_=ap),
                waits=[prog.tok("dma"), prog.tok("act"), prog.tok("dve"), prog.tok("pe"), prog.tok("gp")])
    for e2 in ENGINES:
        waits = [prog.tok(s) for s in ("dma", "pe", "act", "dve", "gp")]
        prog.op(e2, lambda e: None, waits)
    with es:
        with nc.Block() as block:
            for name in ENGINES:
                def make(name):
                    def emit(eng):
                        for fn in prog.streams[name]:
                            fn(eng)
                    return emit
                getattr(block, name)(make(name))


_CACHE = {}


def kernel(L0_attn_x, L0_attn_in_proj_w, L0_attn_conv_w, L0_attn_conv_b, L0_attn_x_proj_w,
           L0_attn_dt_proj_w, L0_attn_dt_proj_b, L0_attn_A_log, L0_attn_D_param,
           L0_attn_out_proj_w, L0_ffn_w1, L0_ffn_b1, L0_ffn_w2, L0_ffn_b2):
    x = np.asarray(L0_attn_x)
    B, L, _ = x.shape
    x4d = np.ascontiguousarray(x.reshape(B * L // TS, TS, DM))
    c8 = np.ascontiguousarray

    if "nc" not in _CACHE:
        _CACHE["nc"] = build_program()
    nc = _CACHE["nc"]

    in_maps = []
    ident = np.eye(128, dtype=np.float32)
    w1_h = c8(L0_ffn_w1)
    w2_h = c8(L0_ffn_w2)
    b1r = c8(np.asarray(L0_ffn_b1).reshape(32, 128).T)
    b2r = c8(np.asarray(L0_ffn_b2).reshape(8, 128).T)
    woutT_h = c8(np.asarray(L0_attn_out_proj_w).T)
    for c in range(NCORE):
        sl = slice(c * DSL, (c + 1) * DSL)
        winT_h = c8(np.concatenate([L0_attn_in_proj_w[sl, :],
                                    L0_attn_in_proj_w[DI + c * DSL: DI + (c + 1) * DSL, :]],
                                   axis=0).T)
        wxpT_h = c8(np.asarray(L0_attn_x_proj_w)[:, sl].T)
        wdtT_h = c8(np.asarray(L0_attn_dt_proj_w)[sl, :].T)
        convw_r = c8(np.asarray(L0_attn_conv_w)[sl].reshape(2, 128, 4).transpose(1, 0, 2).reshape(128, 8))
        alog_r = c8(np.asarray(L0_attn_A_log)[sl].reshape(2, 128, 16).transpose(1, 0, 2).reshape(128, 32))
        cols = np.zeros((128, 8), np.float32)
        cols[:, 0:2] = np.asarray(L0_attn_conv_b)[sl].reshape(2, 128).T
        cols[:, 2:4] = np.asarray(L0_attn_dt_proj_b)[sl].reshape(2, 128).T
        cols[:, 4:6] = np.asarray(L0_attn_D_param)[sl].reshape(2, 128).T
        xs_c = c8(x4d[[j * 8 + c for j in range(NJ)]])
        in_maps.append(dict(
            xs=xs_c, w1=w1_h, w2=w2_h, winT=winT_h, wxpT=wxpT_h, wdtT=wdtT_h,
            woutT=woutT_h, b1r=b1r, b2r=b2r, convw_r=convw_r, alog_r=alog_r,
            cols_r=cols, ident=ident))

    res = run_bass_kernel_spmd(nc, in_maps, core_ids=list(range(NCORE)))
    out = np.empty((B * L // TS, TS, DM), np.float32)
    for c in range(NCORE):
        ysc = res.results[c]["ys"]
        for j in range(NJ):
            out[j * 8 + c] = ysc[j]
    return out.reshape(B, L, DM)
